# version 03
# baseline (speedup 1.0000x reference)
import numpy as np
import concourse.bass as bass
import concourse.mybir as mybir
from concourse.bass_utils import run_bass_kernel_spmd

F32 = mybir.dt.float32
BF16 = mybir.dt.bfloat16
I8 = mybir.dt.int8
AF = mybir.ActivationFunctionType
ALU = mybir.AluOpType

NCORES = 8
S, D = 8192, 4096
TOK = 1024
HALO = 256
TOKH = TOK + 2 * HALO
NI = D // 128
H = 32
DH = 128
HID = 11008
NHID = HID // 128
NH1 = 43
R = 16
SCALE = float(1.0 / np.sqrt(DH))
EPS = 1e-5
INV_D = 1.0 / D

ARENA_BYTES = 206848

OFF_XBF = 0
OFF_ATTN_RD = 0
OFF_WO_RING = 65536
OFF_XLN = 0
OFF_S13_RING = 65536
OFF_SILU = 91392
OFF_S2A_RING = 65536
OFF_P6_XSTAGE = 91392
OFF_S2B_RING = 0
OFF_P8_XSTAGE = 22528
OFF_P8_RES2 = 30720
OFF_P8_SQ = 38912
OFF_P8_STAT = 43008
OFF_P9_RRING = 0
OFF_P9_YOUT = 8192

OFF_COS = 98304
OFF_SINM = 104448
OFF_KT1 = 110592
OFF_KT2 = 116736
OFF_VFM = 122880
OFF_WK_STAGE = 125952
OFF_WV_STAGE = 134144
OFF_SQ_RING = 125952
OFF_QR_RING = 150528
OFF_QTMP1 = 156672
OFF_QTMP2 = 160768
OFF_ATTN_F32 = 164864
OFF_ATTN_BF = 173056
OFF_ATTNT = 177152
OFF_AO = 181248
OFF_XRES_RING = 98304
OFF_RES1_RING = 106496
OFF_SQ1_RING = 114688
OFF_STAT1 = 118784
OFF_XLNF_RING = 139264
OFF_G = 96256

OFF_KR = 184320
OFF_VTM = 187392
OFF_IDENT = 190464
OFF_ONES_F = 190720
OFF_ONES_B = 191232
OFF_ALPHA = 191488
OFF_CORR = 191552
OFF_RBUF = 191616
OFF_CSUB = 191648
OFF_CREC = 191680
OFF_CBUF = 191712
OFF_U1 = 191744
OFF_U3 = 193792
OFF_U2A = 195840
OFF_U2B = 199936
OFF_A13 = 201984
OFF_A2 = 204032

DT_BYTES = {F32: 4, BF16: 2}


def build_nc():
    nc = bass.Bass("TRN2", target_bir_lowering=False)

    xT = nc.dram_tensor("xT", [D, TOKH], F32, kind="ExternalInput").ap()
    wq_t = nc.dram_tensor("wq_t", [NI, 128, NI, 128], F32, kind="ExternalInput").ap()
    wk_t = nc.dram_tensor("wk_t", [1, 128, NI, 128], F32, kind="ExternalInput").ap()
    wv_t = nc.dram_tensor("wv_t", [1, 128, NI, 128], F32, kind="ExternalInput").ap()
    wo_t = nc.dram_tensor("wo_t", [NI, 128, NI, 128], F32, kind="ExternalInput").ap()
    w1_t = nc.dram_tensor("w1_t", [NHID, 128, NI + 1, 128], F32, kind="ExternalInput").ap()
    w3_t = nc.dram_tensor("w3_t", [NHID, 128, NI + 1, 128], F32, kind="ExternalInput").ap()
    w2a_t = nc.dram_tensor("w2a_t", [NI, 128, NH1, 128], F32, kind="ExternalInput").ap()
    w2b_t = nc.dram_tensor("w2b_t", [NI, 128, NH1 + 1, 128], F32, kind="ExternalInput").ap()
    a13_t = nc.dram_tensor("a13_t", [128, NI, 2 * R], F32, kind="ExternalInput").ap()
    a2_t = nc.dram_tensor("a2_t", [128, NHID, R], F32, kind="ExternalInput").ap()
    cos_t = nc.dram_tensor("cos_t", [128, TOKH], F32, kind="ExternalInput").ap()
    sinm_t = nc.dram_tensor("sinm_t", [128, TOKH], F32, kind="ExternalInput").ap()
    alpha_t = nc.dram_tensor("alpha_t", [128, 16], F32, kind="ExternalInput").ap()
    corr_t = nc.dram_tensor("corr_t", [128, 16], F32, kind="ExternalInput").ap()
    ident_t = nc.dram_tensor("ident_t", [128, 128], F32, kind="ExternalInput").ap()
    outT = nc.dram_tensor("outT", [NI, 128, TOK], F32, kind="ExternalOutput").ap()

    attn_park = nc.dram_tensor("attn_park", [NI, 128, TOK], BF16, kind="Internal").ap()
    res1_park = nc.dram_tensor("res1_park", [NI, 128, TOK], F32, kind="Internal").ap()
    xln_park = nc.dram_tensor("xln_park", [NI, 128, TOK], F32, kind="Internal").ap()
    part_park = nc.dram_tensor("part_park", [NI, 128, TOK], F32, kind="Internal").ap()
    res2_park = nc.dram_tensor("res2_park", [NI, 128, TOK], F32, kind="Internal").ap()

    xT_r = xT.rearrange("(i p) m -> p i m", p=128)

    arena = nc.alloc_sbuf_tensor("arena", [128, ARENA_BYTES], I8).ap()

    def view(off, shape, dt, parts=128):
        n = int(np.prod(shape[1:])) * DT_BYTES[dt]
        assert off % 4 == 0 and off + n <= ARENA_BYTES, (off, shape)
        a = arena[0:parts, off:off + n].bitcast(dt)
        if len(shape) > 2:
            names = " ".join(f"d{k}" for k in range(1, len(shape)))
            kw = {f"d{k}": int(shape[k]) for k in range(1, len(shape))}
            a = a.rearrange(f"p ({names}) -> p {names}", **kw)
        return a

    x_bf = view(OFF_XBF, [128, NI, TOKH], BF16)
    cos_k = view(OFF_COS, [128, TOKH], F32)
    sinm_k = view(OFF_SINM, [128, TOKH], F32)
    ktmp1 = view(OFF_KT1, [128, TOKH], F32)
    ktmp2 = view(OFF_KT2, [128, TOKH], F32)
    v_fm = view(OFF_VFM, [128, TOKH], BF16)
    wk_stage = view(OFF_WK_STAGE, [128, NI, 128], BF16)
    wv_stage = view(OFF_WV_STAGE, [128, NI, 128], BF16)
    kr = view(OFF_KR, [128, TOKH], BF16)
    v_tm = view(OFF_VTM, [128, 12, 128], BF16)
    ident = view(OFF_IDENT, [128, 128], BF16)
    ones_f = view(OFF_ONES_F, [128, 128], F32)
    ones_b = view(OFF_ONES_B, [128, 128], BF16)
    alpha = view(OFF_ALPHA, [128, 4, 4], F32)
    corr = view(OFF_CORR, [128, 4, 4], F32)
    rbuf = view(OFF_RBUF, [128, 2, 4], F32)
    csub = view(OFF_CSUB, [128, 2, 4], F32)
    crec = view(OFF_CREC, [128, 2, 4], F32)
    cbuf = view(OFF_CBUF, [128, 2, 4], F32)
    u1_bf = view(OFF_U1, [16, TOK], BF16, parts=16)
    u3_bf = view(OFF_U3, [16, TOK], BF16, parts=16)
    u2a = view(OFF_U2A, [16, TOK], F32, parts=16)
    u2_bf = view(OFF_U2B, [16, TOK], BF16, parts=16)
    a13_bf = view(OFF_A13, [128, NI, 2 * R], BF16)
    a2_bf = view(OFF_A2, [128, NHID, R], BF16)

    stage_q = [view(OFF_SQ_RING + k * 8192, [128, NI, 128], BF16) for k in range(3)]
    qr_ring = view(OFF_QR_RING, [128, 3, TOK], BF16)
    qtmp1 = view(OFF_QTMP1, [128, TOK], F32)
    qtmp2 = view(OFF_QTMP2, [128, TOK], F32)
    attn_f32 = view(OFF_ATTN_F32, [128, 2, 2, 512], F32)
    attn_bfv = view(OFF_ATTN_BF, [128, 2, 2, 512], BF16)
    attnT = view(OFF_ATTNT, [128, 2, 1024], BF16)
    ao_chunk = view(OFF_AO, [128, TOK], BF16)

    attn_rd = view(OFF_ATTN_RD, [128, NI, TOK], BF16)
    wo_stage = [view(OFF_WO_RING + k * 8192, [128, NI, 128], BF16) for k in range(3)]
    xres_ring = [view(OFF_XRES_RING + k * 4096, [128, TOK], F32) for k in range(2)]
    res1_ring = [view(OFF_RES1_RING + k * 4096, [128, TOK], F32) for k in range(2)]
    sq1_ring = [view(OFF_SQ1_RING + k * 2048, [128, TOK], BF16) for k in range(2)]
    st_mean = view(OFF_STAT1, [128, TOK], F32)
    st_msq = view(OFF_STAT1 + 4096, [128, TOK], F32)
    st_var = view(OFF_STAT1 + 8192, [128, TOK], F32)
    st_a = view(OFF_STAT1 + 12288, [128, TOK], F32)
    st_b = view(OFF_STAT1 + 16384, [128, TOK], F32)
    xlnf_ring = [view(OFF_XLNF_RING + k * 4096, [128, TOK], F32) for k in range(2)]

    xln_bf = view(OFF_XLN, [128, NI, TOK], BF16)
    s13_ring = [view(OFF_S13_RING + k * 8448, [128, NI + 1, 128], BF16) for k in range(3)]
    silu_sb = view(OFF_SILU, [128, TOK], F32)
    g_sb = view(OFF_G, [128, NH1, TOK], BF16)
    s2a_ring = [view(OFF_S2A_RING + k * 11008, [128, NH1, 128], BF16) for k in range(2)]
    p6_xstage = view(OFF_P6_XSTAGE, [128, TOK], F32)
    s2b_ring = [view(OFF_S2B_RING + k * 11264, [128, NH1 + 1, 128], BF16) for k in range(2)]
    p8_xstage = [view(OFF_P8_XSTAGE + k * 4096, [128, TOK], F32) for k in range(2)]
    p8_res2 = [view(OFF_P8_RES2 + k * 4096, [128, TOK], F32) for k in range(2)]
    p8_sq = [view(OFF_P8_SQ + k * 2048, [128, TOK], BF16) for k in range(2)]
    st2_mean = view(OFF_P8_STAT, [128, TOK], F32)
    st2_msq = view(OFF_P8_STAT + 4096, [128, TOK], F32)
    st2_var = view(OFF_P8_STAT + 8192, [128, TOK], F32)
    st2_a = view(OFF_P8_STAT + 12288, [128, TOK], F32)
    st2_b = view(OFF_P8_STAT + 16384, [128, TOK], F32)
    p9_rring = [view(OFF_P9_RRING + k * 4096, [128, TOK], F32) for k in range(2)]
    p9_yout = [view(OFF_P9_YOUT + k * 4096, [128, TOK], F32) for k in range(2)]

    WD = nc.semaphore("WD").__enter__()
    PD = nc.semaphore("PD").__enter__()
    MM = nc.semaphore("MM").__enter__()
    VX = nc.semaphore("VX").__enter__()
    SX = nc.semaphore("SX").__enter__()
    SEMS = {"WD": WD, "PD": PD, "MM": MM, "VX": VX, "SX": SX}
    cnt = {k: 0 for k in SEMS}

    OPS = []

    def op(eng, fn, waits=(), inc=None):
        w = [(s, v) for (s, v) in waits if v and v > 0]
        OPS.append((eng, w, fn, inc))
        if inc is not None:
            cnt[inc] += 16 if inc in ("WD", "PD") else 1
            return cnt[inc]
        return None

    def gdma(out, in_, waits=()):
        return op("gpsimd", lambda e, o=out, i=in_: e.dma_start(out=o, in_=i),
                  waits=waits, inc="WD")

    def sdma(out, in_, waits=()):
        return op("sync", lambda e, o=out, i=in_: e.dma_start(out=o, in_=i),
                  waits=waits, inc="PD")

    def psum_open(name, shape, dt):
        g = nc.psum_tensor(name, shape, dt)
        h = g.__enter__()
        return g, h

    gdma(ident[:, :], ident_t)
    gdma(a13_bf[:, :, :], a13_t)
    gdma(a2_bf[:, :, :], a2_t)
    gdma(wk_stage[:, :, :], wk_t[0])
    wd_kv = gdma(wv_stage[:, :, :], wv_t[0])
    for k in range(4):
        wd_x = gdma(x_bf[:, 8 * k:8 * k + 8, :], xT_r[:, 8 * k:8 * k + 8, :])

    sdma(cos_k[:, :], cos_t)
    sdma(sinm_k[:, :], sinm_t)
    sdma(alpha[:, :, :], alpha_t.rearrange("p (a b) -> p a b", a=4, b=4))
    pd_tab = sdma(corr[:, :, :], corr_t.rearrange("p (a b) -> p a b", a=4, b=4))

    def f_ones(e):
        e.memset(ones_f[:, :], 1.0)
        return e.memset(ones_b[:, :], 1.0)
    vx_ones = op("vector", f_ones, inc="VX")

    gK, psK = psum_open("psK", [128, TOKH], F32)
    gV, psV = psum_open("psV", [128, TOKH], F32)
    gVT, psVT = psum_open("psVT", [128, TOKH], BF16)

    def f_kproj(e):
        last = None
        for i in range(NI):
            for s in range(3):
                last = e.matmul(psK[:, 512 * s:512 * s + 512], wk_stage[:, i, :],
                                x_bf[:, i, 512 * s:512 * s + 512],
                                start=(i == 0), stop=(i == NI - 1))
        return last
    mm_k = op("tensor", f_kproj, waits=[("WD", wd_x)], inc="MM")

    def f_vproj(e):
        last = None
        for i in range(NI):
            for s in range(3):
                last = e.matmul(psV[:, 512 * s:512 * s + 512], wv_stage[:, i, :],
                                x_bf[:, i, 512 * s:512 * s + 512],
                                start=(i == 0), stop=(i == NI - 1))
        return last
    mm_v = op("tensor", f_vproj, inc="MM")

    def f_krope(e):
        e.tensor_mul(ktmp1[:, :], psK[:, :], cos_k[:, :])
        e.tensor_mul(ktmp2[0:64, :], psK[64:128, :], sinm_k[0:64, :])
        e.tensor_mul(ktmp2[64:128, :], psK[0:64, :], sinm_k[64:128, :])
        return e.tensor_add(kr[:, :], ktmp1[:, :], ktmp2[:, :])
    vx_kr = op("vector", f_krope, waits=[("MM", mm_k), ("PD", pd_tab)], inc="VX")

    sx_vfm = op("scalar", lambda e: e.copy(v_fm[:, :], psV[:, :]),
                waits=[("MM", mm_v)], inc="SX")

    def f_vt(e):
        last = None
        for k in range(12):
            last = e.transpose(psVT[:, 128 * k:128 * k + 128],
                               v_fm[:, 128 * k:128 * k + 128], ident[:, :])
        return last
    mm_vt = op("tensor", f_vt, waits=[("SX", sx_vfm), ("WD", wd_kv)], inc="MM")

    vx_vtm = op("vector",
                lambda e: e.tensor_copy(
                    v_tm[:, :, :],
                    psVT[:, 0:1536].rearrange("p (a b) -> p a b", a=12, b=128)),
                waits=[("MM", mm_vt)], inc="VX")

    gVT.__exit__(None, None, None)
    gV.__exit__(None, None, None)
    gK.__exit__(None, None, None)

    gQ, psQ = psum_open("psQ", [128, TOK], F32)
    gS0, psS0 = psum_open("psS0", [128, TOK], F32)
    gS1, psS1 = psum_open("psS1", [128, TOK], F32)
    gT, psT = psum_open("psT", [128, TOK], BF16)
    gAV, psAV = psum_open("psAV", [128, 512], F32)
    psS = [psS0, psS1]

    mm_q = [0] * H
    vx_rope = [0] * H
    qr_last_scores = [0] * H
    pd_park = [0] * H
    psS_last_sx = [0, 0]
    af_last_vx = [0, 0]
    abf_last_mm = [0, 0]
    at_last_mm = [0, 0]
    psT_last_vx = 0
    psav_last_vx = 0
    rbuf_last_vx = [0, 0]
    c_of = {}

    cosq = cos_k[:, HALO:HALO + TOK]
    sinq = sinm_k[:, HALO:HALO + TOK]

    for h in range(H):
        if h < 2:
            w_st = [("MM", mm_v)]
        elif h == 2:
            w_st = []
        else:
            w_st = [("MM", mm_q[h - 3])]
        wd_q = gdma(stage_q[h % 3][:, :, :], wq_t[h], waits=w_st)

        def f_qproj(e, h=h, sl=stage_q[h % 3]):
            last = None
            for i in range(NI):
                for th in range(2):
                    last = e.matmul(psQ[:, 512 * th:512 * th + 512], sl[:, i, :],
                                    x_bf[:, i, HALO + 512 * th:HALO + 512 * th + 512],
                                    start=(i == 0), stop=(i == NI - 1))
            return last
        wq_waits = [("WD", wd_q)]
        if h > 0:
            wq_waits.append(("VX", vx_rope[h - 1]))
        mm_q[h] = op("tensor", f_qproj, waits=wq_waits, inc="MM")

        def f_rope(e, h=h):
            e.tensor_mul(qtmp1[:, :], psQ[:, :], cosq)
            e.tensor_mul(qtmp2[0:64, :], psQ[64:128, :], sinq[0:64, :])
            e.tensor_mul(qtmp2[64:128, :], psQ[0:64, :], sinq[64:128, :])
            return e.tensor_add(qr_ring[:, h % 3, :], qtmp1[:, :], qtmp2[:, :])
        rope_waits = [("MM", mm_q[h])]
        if h >= 3:
            rope_waits.append(("MM", qr_last_scores[h - 3]))
        vx_rope[h] = op("vector", f_rope, waits=rope_waits, inc="VX")

        for j in range(4):
            mm_sc = [0, 0]
            sx_sig = [0, 0]
            for w in range(2):
                koff = 256 * j + 256 * w

                def f_scores(e, h=h, j=j, w=w, koff=koff):
                    last = None
                    for qc in range(2):
                        qo = 256 * j + 128 * qc
                        last = e.matmul(psS[w][:, 512 * qc:512 * qc + 512],
                                        qr_ring[:, h % 3, qo:qo + 128],
                                        kr[:, koff:koff + 512],
                                        start=True, stop=True)
                    return last
                sc_waits = [("VX", vx_rope[h]), ("SX", psS_last_sx[w])]
                mm_sc[w] = op("tensor", f_scores, waits=sc_waits, inc="MM")

                def f_sig(e, h=h, j=j, w=w):
                    last = None
                    for qc in range(2):
                        last = e.activation(attn_f32[:, w, qc, :],
                                            psS[w][:, 512 * qc:512 * qc + 512],
                                            AF.Sigmoid, scale=SCALE,
                                            accum_out=rbuf[:, j % 2, 2 * w + qc:2 * w + qc + 1])
                    return last
                sig_waits = [("MM", mm_sc[w]), ("VX", af_last_vx[w]),
                             ("VX", rbuf_last_vx[j % 2])]
                sx_sig[w] = op("scalar", f_sig, waits=sig_waits, inc="SX")
                psS_last_sx[w] = sx_sig[w]

            vx_c1 = op("vector",
                       lambda e, j=j: e.tensor_sub(csub[:, j % 2, :],
                                                   rbuf[:, j % 2, :], corr[:, j, :]),
                       waits=[("SX", sx_sig[1])], inc="VX")
            vx_c2 = op("vector",
                       lambda e, j=j: e.reciprocal(crec[:, j % 2, :],
                                                   csub[:, j % 2, :]),
                       waits=[("VX", vx_c1)], inc="VX")
            vx_c = op("vector",
                      lambda e, j=j: e.tensor_mul(cbuf[:, j % 2, :],
                                                  crec[:, j % 2, :], alpha[:, j, :]),
                      waits=[("VX", vx_c2)], inc="VX")
            rbuf_last_vx[j % 2] = vx_c
            c_of[(h, j)] = vx_c

            mm_av_last = 0
            for w in range(2):
                def f_scale(e, h=h, j=j, w=w):
                    last = None
                    for qc in range(2):
                        last = e.tensor_scalar(attn_bfv[:, w, qc, :],
                                               attn_f32[:, w, qc, :],
                                               cbuf[:, j % 2, 2 * w + qc:2 * w + qc + 1],
                                               None, ALU.mult)
                    return last
                vx_scale = op("vector", f_scale,
                              waits=[("VX", vx_c), ("MM", abf_last_mm[w])], inc="VX")
                af_last_vx[w] = vx_scale

                def f_tp(e, w=w):
                    last = None
                    for ks in range(4):
                        for qc in range(2):
                            last = e.transpose(
                                psT[:, (2 * ks + qc) * 128:(2 * ks + qc + 1) * 128],
                                attn_bfv[:, w, qc, 128 * ks:128 * ks + 128],
                                ident[:, :])
                    return last
                mm_tp = op("tensor", f_tp,
                           waits=[("VX", vx_scale), ("VX", psT_last_vx)], inc="MM")
                abf_last_mm[w] = mm_tp

                vx_te = op("vector",
                           lambda e, w=w: e.tensor_copy(attnT[:, w, :], psT[:, 0:1024]),
                           waits=[("MM", mm_tp), ("MM", at_last_mm[w])], inc="VX")
                psT_last_vx = vx_te

                def f_av(e, j=j, w=w):
                    last = None
                    for ks in range(4):
                        last = e.matmul(psAV[:, 0:256], v_tm[:, 2 * j + 2 * w + ks, :],
                                        attnT[:, w, 256 * ks:256 * ks + 256],
                                        start=(w == 0 and ks == 0),
                                        stop=(w == 1 and ks == 3))
                    return last
                av_waits = [("VX", vx_te)]
                if w == 0:
                    av_waits.append(("VX", psav_last_vx))
                mm_av_last = op("tensor", f_av, waits=av_waits, inc="MM")

            at_last_mm[0] = mm_av_last
            at_last_mm[1] = mm_av_last
            qr_last_scores[h] = mm_sc[1]

            ave_waits = [("MM", mm_av_last)]
            if j == 0 and h > 0:
                ave_waits.append(("PD", pd_park[h - 1]))
            vx_ave = op("vector",
                        lambda e, j=j: e.tensor_copy(ao_chunk[:, 256 * j:256 * j + 256],
                                                     psAV[:, 0:256]),
                        waits=ave_waits, inc="VX")
            psav_last_vx = vx_ave

        pd_park[h] = sdma(attn_park[h], ao_chunk[:, :], waits=[("VX", psav_last_vx)])

    gAV.__exit__(None, None, None)
    gT.__exit__(None, None, None)
    gS1.__exit__(None, None, None)
    gS0.__exit__(None, None, None)
    gQ.__exit__(None, None, None)

    gWo, psWo = psum_open("psWo", [128, TOK], F32)
    gSum, psSum = psum_open("psSum", [128, TOK], F32)
    gSq, psSq = psum_open("psSq", [128, TOK], F32)

    pd_ard = 0
    for o in range(NI):
        w = [("MM", mm_q[H - 1])] if o == 0 else []
        pd_ard = sdma(attn_rd[:, o, :], attn_park[o], waits=w)

    mm_wo = [0] * NI
    vx_res1 = [0] * NI
    sx_sq = [0] * NI
    mm_stats = [0] * NI
    pd_r1park = [0] * NI
    wd_wo = [0] * NI
    pd_xres = [0] * NI

    for o in range(NI):
        w_st = [("MM", mm_q[H - 1])] if o < 3 else [("MM", mm_wo[o - 3])]
        wd_wo[o] = gdma(wo_stage[o % 3][:, :, :], wo_t[o], waits=w_st)

        xres_waits = []
        if o < 2:
            xres_waits += [("VX", vx_rope[H - 1]), ("MM", mm_q[H - 1])]
        else:
            xres_waits.append(("VX", vx_res1[o - 2]))
        pd_xres[o] = sdma(xres_ring[o % 2][:, :], xT_r[:, o, HALO:HALO + TOK],
                          waits=xres_waits)

        def f_wo(e, o=o, sl=wo_stage[o % 3]):
            last = None
            for i in range(NI):
                for th in range(2):
                    last = e.matmul(psWo[:, 512 * th:512 * th + 512], sl[:, i, :],
                                    attn_rd[:, i, 512 * th:512 * th + 512],
                                    start=(i == 0), stop=(i == NI - 1))
            return last
        wo_waits = [("WD", wd_wo[o])]
        if o == 0:
            wo_waits.append(("PD", pd_ard))
        if o > 0:
            wo_waits.append(("VX", vx_res1[o - 1]))
        mm_wo[o] = op("tensor", f_wo, waits=wo_waits, inc="MM")

        r1_waits = [("MM", mm_wo[o]), ("PD", pd_xres[o])]
        if o >= 2:
            r1_waits += [("PD", pd_r1park[o - 2]), ("MM", mm_stats[o - 2]),
                         ("SX", sx_sq[o - 2])]
        vx_res1[o] = op("vector",
                        lambda e, o=o: e.tensor_add(res1_ring[o % 2][:, :],
                                                    psWo[:, :], xres_ring[o % 2][:, :]),
                        waits=r1_waits, inc="VX")

        sq_waits = [("VX", vx_res1[o])]
        if o >= 2:
            sq_waits.append(("MM", mm_stats[o - 2]))
        sx_sq[o] = op("scalar",
                      lambda e, o=o: e.activation(sq1_ring[o % 2][:, :],
                                                  res1_ring[o % 2][:, :], AF.Square),
                      waits=sq_waits, inc="SX")

        def f_stats(e, o=o):
            for th in range(2):
                e.matmul(psSum[:, 512 * th:512 * th + 512], ones_f[:, :],
                         res1_ring[o % 2][:, 512 * th:512 * th + 512],
                         start=(o == 0), stop=(o == NI - 1))
            last = None
            for th in range(2):
                last = e.matmul(psSq[:, 512 * th:512 * th + 512], ones_b[:, :],
                                sq1_ring[o % 2][:, 512 * th:512 * th + 512],
                                start=(o == 0), stop=(o == NI - 1))
            return last
        stat_waits = [("VX", vx_res1[o]), ("SX", sx_sq[o])]
        if o == 0:
            stat_waits.append(("VX", vx_ones))
        mm_stats[o] = op("tensor", f_stats, waits=stat_waits, inc="MM")

        pd_r1park[o] = sdma(res1_park[o], res1_ring[o % 2][:, :],
                            waits=[("VX", vx_res1[o])])

    def f_coef1a(e):
        e.tensor_scalar(st_mean[:, :], psSum[:, :], INV_D, None, ALU.mult)
        e.tensor_scalar(st_msq[:, :], psSq[:, :], INV_D, None, ALU.mult)
        e.tensor_mul(st_var[:, :], st_mean[:, :], st_mean[:, :])
        e.tensor_sub(st_var[:, :], st_msq[:, :], st_var[:, :])
        return e.tensor_scalar(st_a[:, :], st_var[:, :], EPS, None, ALU.add)
    vx_c1a = op("vector", f_coef1a, waits=[("MM", mm_stats[NI - 1])], inc="VX")
    sx_c1 = op("scalar", lambda e: e.activation(st_var[:, :], st_a[:, :], AF.Sqrt),
               waits=[("VX", vx_c1a)], inc="SX")

    def f_coef1b(e):
        e.reciprocal(st_a[:, :], st_var[:, :])
        return e.tensor_mul(st_b[:, :], st_mean[:, :], st_a[:, :])
    vx_ab1 = op("vector", f_coef1b, waits=[("SX", sx_c1)], inc="VX")

    gSq.__exit__(None, None, None)
    gSum.__exit__(None, None, None)
    gWo.__exit__(None, None, None)

    gU1p, psU1 = psum_open("psU1", [16, TOK], F32)
    gU3p, psU3 = psum_open("psU3", [16, TOK], F32)

    pd_r1r = [0] * NI
    vx_xln = [0] * NI
    sx_xbf = [0] * NI
    pd_xlnpark = [0] * NI
    mm_u13 = [0] * NI

    for o in range(NI):
        rr_waits = []
        if o < 2:
            rr_waits += [("MM", mm_stats[NI - 1]), ("SX", sx_sq[NI - 1])]
        else:
            rr_waits.append(("VX", vx_xln[o - 2]))
        pd_r1r[o] = sdma(res1_ring[o % 2][:, :], res1_park[o], waits=rr_waits)

        def f_apply(e, o=o):
            e.tensor_mul(qtmp1[:, :], res1_ring[o % 2][:, :], st_a[:, :])
            return e.tensor_sub(xlnf_ring[o % 2][:, :], qtmp1[:, :], st_b[:, :])
        ap_waits = [("PD", pd_r1r[o])]
        if o == 0:
            ap_waits.append(("VX", vx_ab1))
        if o >= 2:
            ap_waits.append(("PD", pd_xlnpark[o - 2]))
        vx_xln[o] = op("vector", f_apply, waits=ap_waits, inc="VX")

        xbf_waits = [("VX", vx_xln[o])]
        if o == 0:
            xbf_waits.append(("MM", mm_stats[NI - 1]))
        sx_xbf[o] = op("scalar",
                       lambda e, o=o: e.copy(xln_bf[:, o, :], xlnf_ring[o % 2][:, :]),
                       waits=xbf_waits, inc="SX")

        pd_xlnpark[o] = sdma(xln_park[o], xlnf_ring[o % 2][:, :],
                             waits=[("VX", vx_xln[o])])

        def f_u13(e, o=o):
            last = None
            for th in range(2):
                e.matmul(psU1[:, 512 * th:512 * th + 512], a13_bf[:, o, 0:16],
                         xln_bf[:, o, 512 * th:512 * th + 512],
                         start=(o == 0), stop=(o == NI - 1))
                last = e.matmul(psU3[:, 512 * th:512 * th + 512], a13_bf[:, o, 16:32],
                                xln_bf[:, o, 512 * th:512 * th + 512],
                                start=(o == 0), stop=(o == NI - 1))
            return last
        mm_u13[o] = op("tensor", f_u13, waits=[("SX", sx_xbf[o])], inc="MM")

    def f_u13e(e):
        e.tensor_scalar(u1_bf[:, :], psU1[:, :], 2.0, None, ALU.mult)
        return e.tensor_scalar(u3_bf[:, :], psU3[:, :], 2.0, None, ALU.mult)
    vx_u13 = op("vector", f_u13e, waits=[("MM", mm_u13[NI - 1])], inc="VX")

    gU3p.__exit__(None, None, None)
    gU1p.__exit__(None, None, None)

    gW2, psW2 = psum_open("psW2", [128, TOK], F32)
    gU2, psU2 = psum_open("psU2", [16, TOK], F32)
    gH1, psH1 = psum_open("psH1", [128, TOK], F32)
    gH3, psH3 = psum_open("psH3", [128, TOK], F32)

    mm_h = {}
    vx_g = {}
    sx_silu = {}
    slot_last_mm = {}

    def emit_h_phase(o_range, first_waits, prev_silu_sx, prev_g_vx, silu_pd_guard=0):
        for oi, o in enumerate(o_range):
            oo = o % NH1
            s1 = (2 * o) % 3
            s3 = (2 * o + 1) % 3
            wd1 = gdma(s13_ring[s1][:, :, :], w1_t[o],
                       waits=[("MM", slot_last_mm.get(("s13", s1), 0))])
            wd3 = gdma(s13_ring[s3][:, :, :], w3_t[o],
                       waits=[("MM", slot_last_mm.get(("s13", s3), 0))])

            def f_h1(e, o=o, s=s1):
                last = None
                for th in range(2):
                    for i in range(NI):
                        e.matmul(psH1[:, 512 * th:512 * th + 512],
                                 s13_ring[s][:, i, :],
                                 xln_bf[:, i, 512 * th:512 * th + 512],
                                 start=(i == 0), stop=False)
                    last = e.matmul(psH1[:, 512 * th:512 * th + 512],
                                    s13_ring[s][0:16, NI, :],
                                    u1_bf[:, 512 * th:512 * th + 512],
                                    start=False, stop=True)
                return last
            h1_waits = [("WD", wd1)]
            if oi > 0:
                h1_waits.append(("VX", vx_g[o_range[oi - 1]]))
            else:
                h1_waits += first_waits
                h1_waits += [("SX", prev_silu_sx), ("VX", prev_g_vx)]
            mm1 = op("tensor", f_h1, waits=h1_waits, inc="MM")
            slot_last_mm[("s13", s1)] = mm1

            def f_h3(e, o=o, s=s3):
                last = None
                for th in range(2):
                    for i in range(NI):
                        e.matmul(psH3[:, 512 * th:512 * th + 512],
                                 s13_ring[s][:, i, :],
                                 xln_bf[:, i, 512 * th:512 * th + 512],
                                 start=(i == 0), stop=False)
                    last = e.matmul(psH3[:, 512 * th:512 * th + 512],
                                    s13_ring[s][0:16, NI, :],
                                    u3_bf[:, 512 * th:512 * th + 512],
                                    start=False, stop=True)
                return last
            mm3 = op("tensor", f_h3, waits=[("WD", wd3)], inc="MM")
            slot_last_mm[("s13", s3)] = mm3
            mm_h[o] = mm3

            si_waits = [("MM", mm1)]
            if oi > 0:
                si_waits.append(("VX", vx_g[o_range[oi - 1]]))
            else:
                si_waits.append(("VX", prev_g_vx))
                if silu_pd_guard:
                    si_waits.append(("PD", silu_pd_guard))
            sx_silu[o] = op("scalar",
                            lambda e, o=o: e.activation(silu_sb[:, :], psH1[:, :], AF.Silu),
                            waits=si_waits, inc="SX")

            g_waits = [("SX", sx_silu[o]), ("MM", mm3)]
            if oi == 0:
                g_waits += first_waits
            vx_g[o] = op("vector",
                         lambda e, oo=oo: e.tensor_mul(g_sb[:, oo, :], silu_sb[:, :],
                                                       psH3[:, :]),
                         waits=g_waits, inc="VX")

    for k in range(3):
        slot_last_mm[("s13", k)] = 0
    p5_first = [("PD", pd_xlnpark[NI - 1]), ("MM", mm_u13[NI - 1]), ("VX", vx_u13)]
    slot_last_mm[("s13", 0)] = mm_wo[NI - 1]
    slot_last_mm[("s13", 1)] = mm_wo[NI - 1]
    slot_last_mm[("s13", 2)] = mm_wo[NI - 1]
    emit_h_phase(list(range(NH1)), p5_first, prev_silu_sx=0, prev_g_vx=0)

    def f_u2a(e):
        last = None
        for th in range(2):
            for i in range(NH1):
                last = e.matmul(psU2[:, 512 * th:512 * th + 512], a2_bf[:, i, :],
                                g_sb[:, i, 512 * th:512 * th + 512],
                                start=(i == 0), stop=(i == NH1 - 1))
        return last
    mm_u2a = op("tensor", f_u2a, waits=[("VX", vx_g[NH1 - 1])], inc="MM")
    vx_u2a = op("vector",
                lambda e: e.tensor_scalar(u2a[:, :], psU2[:, :], 2.0, None, ALU.mult),
                waits=[("MM", mm_u2a)], inc="VX")

    vx_part = [0] * NI
    mm_w2a = [0] * NI
    pd_pp = [0] * NI
    s2a_last = [mm_h[NH1 - 1], mm_h[NH1 - 1]]

    for o in range(NI):
        wd2 = gdma(s2a_ring[o % 2][:, :, :], w2a_t[o], waits=[("MM", s2a_last[o % 2])])

        xr_waits = [("VX", vx_g[NH1 - 1])] if o == 0 else [("VX", vx_part[o - 1])]
        pd_xr = sdma(p6_xstage[:, :], xln_park[o], waits=xr_waits)

        def f_w2a(e, o=o, sl=s2a_ring[o % 2]):
            last = None
            for th in range(2):
                for i in range(NH1):
                    last = e.matmul(psW2[:, 512 * th:512 * th + 512], sl[:, i, :],
                                    g_sb[:, i, 512 * th:512 * th + 512],
                                    start=(i == 0), stop=(i == NH1 - 1))
            return last
        w2_waits = [("WD", wd2)]
        if o > 0:
            w2_waits.append(("VX", vx_part[o - 1]))
        mm_w2a[o] = op("tensor", f_w2a, waits=w2_waits, inc="MM")
        s2a_last[o % 2] = mm_w2a[o]

        vx_part[o] = op("vector",
                        lambda e: e.tensor_add(p6_xstage[:, :], psW2[:, :],
                                               p6_xstage[:, :]),
                        waits=[("MM", mm_w2a[o]), ("PD", pd_xr)], inc="VX")
        pd_pp[o] = sdma(part_park[o], p6_xstage[:, :], waits=[("VX", vx_part[o])])

    p7_first = [("MM", mm_w2a[NI - 1])]
    slot_last_mm[("s13", 0)] = mm_w2a[NI - 1]
    slot_last_mm[("s13", 1)] = mm_w2a[NI - 1]
    slot_last_mm[("s13", 2)] = mm_w2a[NI - 1]
    emit_h_phase(list(range(NH1, NHID)), p7_first,
                 prev_silu_sx=sx_silu[NH1 - 1], prev_g_vx=vx_g[NH1 - 1],
                 silu_pd_guard=pd_pp[NI - 1])

    def f_u2d(e):
        last = None
        for th in range(2):
            for i in range(NH1):
                last = e.matmul(psU2[:, 512 * th:512 * th + 512], a2_bf[:, NH1 + i, :],
                                g_sb[:, i, 512 * th:512 * th + 512],
                                start=(i == 0), stop=(i == NH1 - 1))
        return last
    mm_u2d = op("tensor", f_u2d, waits=[("VX", vx_g[NHID - 1])], inc="MM")
    vx_u2 = op("vector",
               lambda e: e.scalar_tensor_tensor(u2_bf[:, :], psU2[:, :], 2.0,
                                                u2a[:, :], ALU.mult, ALU.add),
               waits=[("MM", mm_u2d), ("VX", vx_u2a)], inc="VX")

    gH3.__exit__(None, None, None)
    gH1.__exit__(None, None, None)
    gSum2, psSum2 = psum_open("psSum2", [128, TOK], F32)
    gSq2, psSq2 = psum_open("psSq2", [128, TOK], F32)

    vx_res2 = [0] * NI
    mm_w2b = [0] * NI
    sx_sq2 = [0] * NI
    mm_stats2 = [0] * NI
    pd_r2park = [0] * NI
    s2b_last = [mm_h[NHID - 1], mm_h[NHID - 1]]

    for o in range(NI):
        wd2 = gdma(s2b_ring[o % 2][:, :, :], w2b_t[o], waits=[("MM", s2b_last[o % 2])])

        pr_waits = [("MM", mm_h[NHID - 1])] if o < 2 else [("VX", vx_res2[o - 2])]
        pd_pr = sdma(p8_xstage[o % 2][:, :], part_park[o], waits=pr_waits)

        def f_w2b(e, o=o, sl=s2b_ring[o % 2]):
            last = None
            for th in range(2):
                for i in range(NH1):
                    e.matmul(psW2[:, 512 * th:512 * th + 512], sl[:, i, :],
                             g_sb[:, i, 512 * th:512 * th + 512],
                             start=(i == 0), stop=False)
                last = e.matmul(psW2[:, 512 * th:512 * th + 512],
                                sl[0:16, NH1, :],
                                u2_bf[:, 512 * th:512 * th + 512],
                                start=False, stop=True)
            return last
        w2b_waits = [("WD", wd2), ("VX", vx_u2)]
        if o > 0:
            w2b_waits.append(("VX", vx_res2[o - 1]))
        mm_w2b[o] = op("tensor", f_w2b, waits=w2b_waits, inc="MM")
        s2b_last[o % 2] = mm_w2b[o]

        r2_waits = [("MM", mm_w2b[o]), ("PD", pd_pr)]
        if o >= 2:
            r2_waits += [("PD", pd_r2park[o - 2]), ("MM", mm_stats2[o - 2]),
                         ("SX", sx_sq2[o - 2])]
        vx_res2[o] = op("vector",
                        lambda e, o=o: e.tensor_add(p8_res2[o % 2][:, :], psW2[:, :],
                                                    p8_xstage[o % 2][:, :]),
                        waits=r2_waits, inc="VX")

        sq2_waits = [("VX", vx_res2[o])]
        if o >= 2:
            sq2_waits.append(("MM", mm_stats2[o - 2]))
        sx_sq2[o] = op("scalar",
                       lambda e, o=o: e.activation(p8_sq[o % 2][:, :],
                                                   p8_res2[o % 2][:, :], AF.Square),
                       waits=sq2_waits, inc="SX")

        def f_stats2(e, o=o):
            for th in range(2):
                e.matmul(psSum2[:, 512 * th:512 * th + 512], ones_f[:, :],
                         p8_res2[o % 2][:, 512 * th:512 * th + 512],
                         start=(o == 0), stop=(o == NI - 1))
            last = None
            for th in range(2):
                last = e.matmul(psSq2[:, 512 * th:512 * th + 512], ones_b[:, :],
                                p8_sq[o % 2][:, 512 * th:512 * th + 512],
                                start=(o == 0), stop=(o == NI - 1))
            return last
        mm_stats2[o] = op("tensor", f_stats2,
                          waits=[("VX", vx_res2[o]), ("SX", sx_sq2[o])], inc="MM")

        pd_r2park[o] = sdma(res2_park[o], p8_res2[o % 2][:, :],
                            waits=[("VX", vx_res2[o])])

    def f_coef2a(e):
        e.tensor_scalar(st2_mean[:, :], psSum2[:, :], INV_D, None, ALU.mult)
        e.tensor_scalar(st2_msq[:, :], psSq2[:, :], INV_D, None, ALU.mult)
        e.tensor_mul(st2_var[:, :], st2_mean[:, :], st2_mean[:, :])
        e.tensor_sub(st2_var[:, :], st2_msq[:, :], st2_var[:, :])
        return e.tensor_scalar(st2_a[:, :], st2_var[:, :], EPS, None, ALU.add)
    vx_c2a = op("vector", f_coef2a, waits=[("MM", mm_stats2[NI - 1])], inc="VX")
    sx_c2 = op("scalar", lambda e: e.activation(st2_var[:, :], st2_a[:, :], AF.Sqrt),
               waits=[("VX", vx_c2a)], inc="SX")

    def f_coef2b(e):
        e.reciprocal(st2_a[:, :], st2_var[:, :])
        return e.tensor_mul(st2_b[:, :], st2_mean[:, :], st2_a[:, :])
    vx_ab2 = op("vector", f_coef2b, waits=[("SX", sx_c2)], inc="VX")

    vx_yout = [0] * NI
    pd_out = [0] * NI
    for o in range(NI):
        rr_waits = []
        if o < 2:
            rr_waits.append(("MM", mm_w2b[NI - 1]))
        else:
            rr_waits.append(("VX", vx_yout[o - 2]))
        pd_rr = sdma(p9_rring[o % 2][:, :], res2_park[o], waits=rr_waits)

        def f_y(e, o=o):
            e.tensor_mul(p9_yout[o % 2][:, :], p9_rring[o % 2][:, :], st2_a[:, :])
            return e.tensor_sub(p9_yout[o % 2][:, :], p9_yout[o % 2][:, :], st2_b[:, :])
        y_waits = [("PD", pd_rr)]
        if o == 0:
            y_waits.append(("VX", vx_ab2))
        if o >= 2:
            y_waits.append(("PD", pd_out[o - 2]))
        vx_yout[o] = op("vector", f_y, waits=y_waits, inc="VX")

        pd_out[o] = sdma(outT[o], p9_yout[o % 2][:, :], waits=[("VX", vx_yout[o])])

    final_pd = pd_out[NI - 1]
    op("sync", lambda e: e.nop(), waits=[("PD", final_pd)])

    gSq2.__exit__(None, None, None)
    gSum2.__exit__(None, None, None)
    gU2.__exit__(None, None, None)
    gW2.__exit__(None, None, None)

    with nc.Block() as block:
        for eng_name, starter in (("tensor", block.tensor), ("vector", block.vector),
                                  ("scalar", block.scalar), ("sync", block.sync),
                                  ("gpsimd", block.gpsimd)):
            ops_e = [o for o in OPS if o[0] == eng_name]
            if not ops_e:
                continue

            def body(e, ops_e=ops_e):
                for (_, waits, fn, inc) in ops_e:
                    for (sk, val) in waits:
                        e.wait_ge(SEMS[sk], val)
                    inst = fn(e)
                    if inc is not None:
                        inst.then_inc(SEMS[inc], 16 if inc in ("WD", "PD") else 1)
            starter(body)

    return nc



def _tile_w(W, B=None):
    out_dim, in_dim = W.shape
    O, I = out_dim // 128, in_dim // 128
    t = np.ascontiguousarray(
        W.reshape(O, 128, I, 128).transpose(0, 3, 2, 1).astype(np.float32))
    if B is None:
        return t
    bt = B.reshape(O, 128, R).transpose(0, 2, 1).astype(np.float32)
    pad = np.zeros((O, 128, 1, 128), np.float32)
    pad[:, :R, 0, :] = bt
    return np.ascontiguousarray(np.concatenate([t, pad], axis=2))


def _rope_tables():
    base = 10000.0
    inv_freq = 1.0 / base ** (np.arange(0, DH, 2, dtype=np.float32) / DH)
    t = np.arange(S, dtype=np.float32)
    f = np.outer(t, inv_freq)
    cos = np.zeros((S, DH), dtype=np.float32)
    sin = np.zeros((S, DH), dtype=np.float32)
    cos[:, 0::2] = np.cos(f)
    cos[:, 1::2] = np.cos(f)
    sin[:, 0::2] = np.sin(f)
    sin[:, 1::2] = np.sin(f)
    return cos, sin


def kernel(L0_attn_x, L0_attn_Wq, L0_attn_Wk, L0_attn_Wv, L0_attn_Wo,
           L0_ffn_w1, L0_ffn_w3, L0_ffn_w2,
           L0_ffn_lora_A1, L0_ffn_lora_B1, L0_ffn_lora_A3, L0_ffn_lora_B3,
           L0_ffn_lora_A2, L0_ffn_lora_B2):
    x = np.asarray(L0_attn_x, dtype=np.float32).reshape(S, D)
    Wq = np.asarray(L0_attn_Wq, np.float32)
    Wk = np.asarray(L0_attn_Wk, np.float32)
    Wv = np.asarray(L0_attn_Wv, np.float32)
    Wo = np.asarray(L0_attn_Wo, np.float32)
    w1 = np.asarray(L0_ffn_w1, np.float32)
    w3 = np.asarray(L0_ffn_w3, np.float32)
    w2 = np.asarray(L0_ffn_w2, np.float32)
    A1 = np.asarray(L0_ffn_lora_A1, np.float32)
    B1 = np.asarray(L0_ffn_lora_B1, np.float32)
    A3 = np.asarray(L0_ffn_lora_A3, np.float32)
    B3 = np.asarray(L0_ffn_lora_B3, np.float32)
    A2 = np.asarray(L0_ffn_lora_A2, np.float32)
    B2 = np.asarray(L0_ffn_lora_B2, np.float32)

    wq_t = _tile_w(Wq)
    wk_t = _tile_w(Wk)
    wv_t = _tile_w(Wv)
    wo_t = _tile_w(Wo)
    w1_t = _tile_w(w1, B1)
    w3_t = _tile_w(w3, B3)
    w2a_t = _tile_w(w2[:, :NH1 * 128])
    w2b_t = _tile_w(w2[:, NH1 * 128:], B2)
    A13 = np.concatenate([A1, A3], axis=0)
    a13_t = np.ascontiguousarray(A13.reshape(2 * R, NI, 128).transpose(2, 1, 0))
    a2_t = np.ascontiguousarray(A2.reshape(R, NHID, 128).transpose(2, 1, 0))
    ident = np.eye(128, dtype=np.float32)

    cos_g, sin_g = _rope_tables()
    lin = np.linspace(0.0, 1.0, 256, dtype=np.float32)

    nc = build_nc()
    in_maps = []
    for c in range(NCORES):
        g0 = 1024 * c - HALO
        xh = np.zeros((TOKH, D), np.float32)
        lo, hi = max(0, g0), min(S, g0 + TOKH)
        xh[lo - g0:hi - g0] = x[lo:hi]
        xTc = np.ascontiguousarray(xh.T)

        pos = g0 + np.arange(TOKH)
        valid = (pos >= 0) & (pos < S)
        pc = np.clip(pos, 0, S - 1)
        cos_c = np.where(valid[None, :], cos_g[pc].T, 0.0).astype(np.float32)
        sin_c = np.where(valid[None, :], sin_g[pc].T, 0.0).astype(np.float32)
        sinm_c = sin_c.copy()
        sinm_c[:64] = -sinm_c[:64]

        al = np.zeros((128, 4, 2, 2), np.float32)
        co = np.zeros((128, 4, 2, 2), np.float32)
        p = np.arange(128)
        for j in range(4):
            for qc in range(2):
                u = lin[128 * qc + p]
                al[:, j, 0, qc] = 1.0 - u
                al[:, j, 1, qc] = u
        if c == 0:
            al[:, 0, 0, :] = 0.0
            al[:, 0, 1, :] = 1.0
            co[:, 0, 0, :] = 128.0
        if c == NCORES - 1:
            co[:, 3, 1, :] = 128.0

        in_maps.append({
            "xT": xTc,
            "wq_t": wq_t, "wk_t": wk_t, "wv_t": wv_t, "wo_t": wo_t,
            "w1_t": w1_t, "w3_t": w3_t, "w2a_t": w2a_t, "w2b_t": w2b_t,
            "a13_t": a13_t, "a2_t": a2_t,
            "cos_t": cos_c, "sinm_t": sinm_c,
            "alpha_t": np.ascontiguousarray(al.reshape(128, 16)),
            "corr_t": np.ascontiguousarray(co.reshape(128, 16)),
            "ident_t": ident,
        })

    res = run_bass_kernel_spmd(nc, in_maps, core_ids=list(range(NCORES)))
    parts = []
    for c in range(NCORES):
        oc = np.asarray(res.results[c]["outT"])
        parts.append(oc.reshape(D, TOK).T)
    out = np.concatenate(parts, axis=0).reshape(1, S, D)
    return np.ascontiguousarray(out.astype(np.float32))


# version 09 (final)
# speedup vs baseline: 1.0062x; 1.0062x over previous
import numpy as np
import concourse.bass as bass
import concourse.mybir as mybir
from concourse.bass_utils import run_bass_kernel_spmd

F32 = mybir.dt.float32
BF16 = mybir.dt.bfloat16
I8 = mybir.dt.int8
AF = mybir.ActivationFunctionType
ALU = mybir.AluOpType

NCORES = 8
S, D = 8192, 4096
TOK = 1024
HALO = 256
TOKH = TOK + 2 * HALO
NI = D // 128
H = 32
DH = 128
HID = 11008
NHID = HID // 128
NH1 = 43
R = 16
SCALE = float(1.0 / np.sqrt(DH))
EPS = 1e-5
INV_D = 1.0 / D

ARENA_BYTES = 206848

OFF_XBF = 0
OFF_ATTN_RD = 0
OFF_WO_RING = 65536
OFF_XLN = 0
OFF_S13_RING = 65536
OFF_SILU = 91392
OFF_S2A_RING = 65536
OFF_P6_XSTAGE = 91392
OFF_S2B_RING = 0
OFF_P8_XSTAGE = 22528
OFF_P8_RES2 = 30720
OFF_P8_SQ = 38912
OFF_P8_STAT = 43008
OFF_P9_RRING = 0
OFF_P9_YOUT = 8192

OFF_COS = 98304
OFF_SINM = 104448
OFF_KT1 = 110592
OFF_KT2 = 116736
OFF_VFM = 122880
OFF_WK_STAGE = 125952
OFF_WV_STAGE = 134144
OFF_SQ_RING = 125952
OFF_QR_RING = 150528
OFF_QTMP1 = 156672
OFF_QTMP2 = 160768
OFF_ATTN_F32 = 164864
OFF_ATTN_BF = 173056
OFF_ATTNT = 177152
OFF_AO = 181248
OFF_XRES_RING = 98304
OFF_RES1_RING = 106496
OFF_SQ1_RING = 114688
OFF_STAT1 = 118784
OFF_XLNF_RING = 139264
OFF_G = 96256

OFF_KR = 184320
OFF_VTM = 187392
OFF_IDENT = 190464
OFF_ONES_F = 190720
OFF_ONES_B = 191232
OFF_ALPHA = 191488
OFF_CORR = 191552
OFF_RBUF = 191616
OFF_CSUB = 191648
OFF_CREC = 191680
OFF_CBUF = 191712
OFF_U1 = 191744
OFF_U3 = 193792
OFF_U2A = 195840
OFF_U2B = 199936
OFF_A13 = 201984
OFF_A2 = 204032

DT_BYTES = {F32: 4, BF16: 2}


def build_nc():
    nc = bass.Bass("TRN2", target_bir_lowering=False)

    xT = nc.dram_tensor("xT", [D, TOKH], F32, kind="ExternalInput").ap()
    wq_t = nc.dram_tensor("wq_t", [NI, 128, NI, 128], F32, kind="ExternalInput").ap()
    wk_t = nc.dram_tensor("wk_t", [1, 128, NI, 128], F32, kind="ExternalInput").ap()
    wv_t = nc.dram_tensor("wv_t", [1, 128, NI, 128], F32, kind="ExternalInput").ap()
    wo_t = nc.dram_tensor("wo_t", [NI, 128, NI, 128], F32, kind="ExternalInput").ap()
    w1_t = nc.dram_tensor("w1_t", [NHID, 128, NI + 1, 128], F32, kind="ExternalInput").ap()
    w3_t = nc.dram_tensor("w3_t", [NHID, 128, NI + 1, 128], F32, kind="ExternalInput").ap()
    w2a_t = nc.dram_tensor("w2a_t", [NI, 128, NH1, 128], F32, kind="ExternalInput").ap()
    w2b_t = nc.dram_tensor("w2b_t", [NI, 128, NH1 + 1, 128], F32, kind="ExternalInput").ap()
    a13_t = nc.dram_tensor("a13_t", [128, NI, 2 * R], F32, kind="ExternalInput").ap()
    a2_t = nc.dram_tensor("a2_t", [128, NHID, R], F32, kind="ExternalInput").ap()
    cos_t = nc.dram_tensor("cos_t", [128, TOKH], F32, kind="ExternalInput").ap()
    sinm_t = nc.dram_tensor("sinm_t", [128, TOKH], F32, kind="ExternalInput").ap()
    alpha_t = nc.dram_tensor("alpha_t", [128, 16], F32, kind="ExternalInput").ap()
    corr_t = nc.dram_tensor("corr_t", [128, 16], F32, kind="ExternalInput").ap()
    ident_t = nc.dram_tensor("ident_t", [128, 128], F32, kind="ExternalInput").ap()
    outT = nc.dram_tensor("outT", [NI, 128, TOK], F32, kind="ExternalOutput").ap()

    attn_park = nc.dram_tensor("attn_park", [NI, 128, TOK], BF16, kind="Internal").ap()
    res1_park = nc.dram_tensor("res1_park", [NI, 128, TOK], F32, kind="Internal").ap()
    xln_park = nc.dram_tensor("xln_park", [NI, 128, TOK], F32, kind="Internal").ap()
    part_park = nc.dram_tensor("part_park", [NI, 128, TOK], F32, kind="Internal").ap()
    res2_park = nc.dram_tensor("res2_park", [NI, 128, TOK], F32, kind="Internal").ap()

    xT_r = xT.rearrange("(i p) m -> p i m", p=128)

    arena = nc.alloc_sbuf_tensor("arena", [128, ARENA_BYTES], I8).ap()

    def view(off, shape, dt, parts=128):
        n = int(np.prod(shape[1:])) * DT_BYTES[dt]
        assert off % 4 == 0 and off + n <= ARENA_BYTES, (off, shape)
        a = arena[0:parts, off:off + n].bitcast(dt)
        if len(shape) > 2:
            names = " ".join(f"d{k}" for k in range(1, len(shape)))
            kw = {f"d{k}": int(shape[k]) for k in range(1, len(shape))}
            a = a.rearrange(f"p ({names}) -> p {names}", **kw)
        return a

    x_bf = view(OFF_XBF, [128, NI, TOKH], BF16)
    cos_k = view(OFF_COS, [128, TOKH], F32)
    sinm_k = view(OFF_SINM, [128, TOKH], F32)
    ktmp1 = view(OFF_KT1, [128, TOKH], F32)
    ktmp2 = view(OFF_KT2, [128, TOKH], F32)
    v_fm = view(OFF_VFM, [128, TOKH], BF16)
    wk_stage = view(OFF_WK_STAGE, [128, NI, 128], BF16)
    wv_stage = view(OFF_WV_STAGE, [128, NI, 128], BF16)
    kr = view(OFF_KR, [128, TOKH], BF16)
    v_tm = view(OFF_VTM, [128, 12, 128], BF16)
    ident = view(OFF_IDENT, [128, 128], BF16)
    ones_f = view(OFF_ONES_F, [128, 128], F32)
    ones_b = view(OFF_ONES_B, [128, 128], BF16)
    alpha = view(OFF_ALPHA, [128, 4, 4], F32)
    corr = view(OFF_CORR, [128, 4, 4], F32)
    rbuf = view(OFF_RBUF, [128, 2, 4], F32)
    csub = view(OFF_CSUB, [128, 2, 4], F32)
    crec = view(OFF_CREC, [128, 2, 4], F32)
    cbuf = view(OFF_CBUF, [128, 2, 4], F32)
    u1_bf = view(OFF_U1, [16, TOK], BF16, parts=16)
    u3_bf = view(OFF_U3, [16, TOK], BF16, parts=16)
    u2a = view(OFF_U2A, [16, TOK], F32, parts=16)
    u2_bf = view(OFF_U2B, [16, TOK], BF16, parts=16)
    a13_bf = view(OFF_A13, [128, NI, 2 * R], BF16)
    a2_bf = view(OFF_A2, [128, NHID, R], BF16)

    stage_q = [view(OFF_SQ_RING + k * 8192, [128, NI, 128], BF16) for k in range(3)]
    qr_ring = view(OFF_QR_RING, [128, 3, TOK], BF16)
    qtmp1 = view(OFF_QTMP1, [128, TOK], F32)
    qtmp2 = view(OFF_QTMP2, [128, TOK], F32)
    attn_f32 = view(OFF_ATTN_F32, [128, 2, 2, 512], F32)
    attn_bfv = view(OFF_ATTN_BF, [128, 2, 2, 512], BF16)
    attnT = view(OFF_ATTNT, [128, 2, 1024], BF16)
    ao_chunk = view(OFF_AO, [128, TOK], BF16)

    attn_rd = view(OFF_ATTN_RD, [128, NI, TOK], BF16)
    wo_stage = [view(OFF_WO_RING + k * 8192, [128, NI, 128], BF16) for k in range(3)]
    xres_ring = [view(OFF_XRES_RING + k * 4096, [128, TOK], F32) for k in range(2)]
    res1_ring = [view(OFF_RES1_RING + k * 4096, [128, TOK], F32) for k in range(2)]
    sq1_ring = [view(OFF_SQ1_RING + k * 2048, [128, TOK], BF16) for k in range(2)]
    st_mean = view(OFF_STAT1, [128, TOK], F32)
    st_msq = view(OFF_STAT1 + 4096, [128, TOK], F32)
    st_var = view(OFF_STAT1 + 8192, [128, TOK], F32)
    st_a = view(OFF_STAT1 + 12288, [128, TOK], F32)
    st_b = view(OFF_STAT1 + 16384, [128, TOK], F32)
    xlnf_ring = [view(OFF_XLNF_RING + k * 4096, [128, TOK], F32) for k in range(2)]

    xln_bf = view(OFF_XLN, [128, NI, TOK], BF16)
    s13_ring = [view(OFF_S13_RING + k * 8448, [128, NI + 1, 128], BF16) for k in range(3)]
    silu_sb = view(OFF_SILU, [128, TOK], F32)
    g_sb = view(OFF_G, [128, NH1, TOK], BF16)
    s2a_ring = [view(OFF_S2A_RING + k * 11008, [128, NH1, 128], BF16) for k in range(2)]
    p6_xstage = view(OFF_P6_XSTAGE, [128, TOK], F32)
    s2b_ring = [view(OFF_S2B_RING + k * 11264, [128, NH1 + 1, 128], BF16) for k in range(2)]
    p8_xstage = [view(OFF_P8_XSTAGE + k * 4096, [128, TOK], F32) for k in range(2)]
    p8_res2 = [view(OFF_P8_RES2 + k * 4096, [128, TOK], F32) for k in range(2)]
    p8_sq = [view(OFF_P8_SQ + k * 2048, [128, TOK], BF16) for k in range(2)]
    st2_mean = view(OFF_P8_STAT, [128, TOK], F32)
    st2_msq = view(OFF_P8_STAT + 4096, [128, TOK], F32)
    st2_var = view(OFF_P8_STAT + 8192, [128, TOK], F32)
    st2_a = view(OFF_P8_STAT + 12288, [128, TOK], F32)
    st2_b = view(OFF_P8_STAT + 16384, [128, TOK], F32)
    p9_rring = [view(OFF_P9_RRING + k * 4096, [128, TOK], F32) for k in range(2)]
    p9_yout = [view(OFF_P9_YOUT + k * 4096, [128, TOK], F32) for k in range(2)]

    WD = nc.semaphore("WD").__enter__()
    PD = nc.semaphore("PD").__enter__()
    MM = nc.semaphore("MM").__enter__()
    VX = nc.semaphore("VX").__enter__()
    SX = nc.semaphore("SX").__enter__()
    SEMS = {"WD": WD, "PD": PD, "MM": MM, "VX": VX, "SX": SX}
    cnt = {k: 0 for k in SEMS}

    OPS = []

    def op(eng, fn, waits=(), inc=None):
        w = [(s, v) for (s, v) in waits if v and v > 0]
        OPS.append((eng, w, fn, inc))
        if inc is not None:
            cnt[inc] += 16 if inc in ("WD", "PD") else 1
            return cnt[inc]
        return None

    def gdma(out, in_, waits=()):
        return op("gpsimd", lambda e, o=out, i=in_: e.dma_start(out=o, in_=i),
                  waits=waits, inc="WD")

    def sdma(out, in_, waits=()):
        return op("sync", lambda e, o=out, i=in_: e.dma_start(out=o, in_=i),
                  waits=waits, inc="PD")

    def psum_open(name, shape, dt):
        g = nc.psum_tensor(name, shape, dt)
        h = g.__enter__()
        return g, h

    gdma(ident[:, :], ident_t)
    gdma(a13_bf[:, :, :], a13_t)
    gdma(a2_bf[:, :, :], a2_t)
    gdma(wk_stage[:, :, :], wk_t[0])
    wd_kv = gdma(wv_stage[:, :, :], wv_t[0])
    wd_xs = [0] * 4
    for k in range(4):
        wd_xs[k] = gdma(x_bf[:, 8 * k:8 * k + 8, :], xT_r[:, 8 * k:8 * k + 8, :])
    wd_x = wd_xs[3]

    sdma(cos_k[:, :], cos_t)
    sdma(sinm_k[:, :], sinm_t)
    sdma(alpha[:, :, :], alpha_t.rearrange("p (a b) -> p a b", a=4, b=4))
    pd_tab = sdma(corr[:, :, :], corr_t.rearrange("p (a b) -> p a b", a=4, b=4))

    def f_ones(e):
        e.memset(ones_f[:, :], 1.0)
        return e.memset(ones_b[:, :], 1.0)
    vx_ones = op("vector", f_ones, inc="VX")

    gK, psK = psum_open("psK", [128, TOKH], F32)
    gV, psV = psum_open("psV", [128, TOKH], F32)
    gVT, psVT = psum_open("psVT", [128, TOKH], BF16)

    mm_k = 0
    for kx in range(4):
        def f_kproj(e, kx=kx):
            last = None
            for i in range(8 * kx, 8 * kx + 8):
                for s in range(3):
                    last = e.matmul(psK[:, 512 * s:512 * s + 512], wk_stage[:, i, :],
                                    x_bf[:, i, 512 * s:512 * s + 512],
                                    start=(i == 0), stop=(i == NI - 1))
            return last
        mm_k = op("tensor", f_kproj, waits=[("WD", wd_xs[kx])], inc="MM")

    mm_v = 0
    for kx in range(4):
        def f_vproj(e, kx=kx):
            last = None
            for i in range(8 * kx, 8 * kx + 8):
                for s in range(3):
                    last = e.matmul(psV[:, 512 * s:512 * s + 512], wv_stage[:, i, :],
                                    x_bf[:, i, 512 * s:512 * s + 512],
                                    start=(i == 0), stop=(i == NI - 1))
            return last
        mm_v = op("tensor", f_vproj, waits=[("WD", wd_kv)] if kx == 0 else [],
                  inc="MM")

    def f_krope(e):
        e.tensor_mul(ktmp1[:, :], psK[:, :], cos_k[:, :])
        e.tensor_mul(ktmp2[0:64, :], psK[64:128, :], sinm_k[0:64, :])
        e.tensor_mul(ktmp2[64:128, :], psK[0:64, :], sinm_k[64:128, :])
        return e.tensor_add(kr[:, :], ktmp1[:, :], ktmp2[:, :])
    vx_kr = op("vector", f_krope, waits=[("MM", mm_k), ("PD", pd_tab)], inc="VX")

    sx_vfm = op("scalar", lambda e: e.copy(v_fm[:, :], psV[:, :]),
                waits=[("MM", mm_v)], inc="SX")

    def f_vt(e):
        last = None
        for k in range(12):
            last = e.transpose(psVT[:, 128 * k:128 * k + 128],
                               v_fm[:, 128 * k:128 * k + 128], ident[:, :])
        return last
    mm_vt = op("tensor", f_vt, waits=[("SX", sx_vfm), ("WD", wd_kv)], inc="MM")

    vx_vtm = op("vector",
                lambda e: e.tensor_copy(
                    v_tm[:, :, :],
                    psVT[:, 0:1536].rearrange("p (a b) -> p a b", a=12, b=128)),
                waits=[("MM", mm_vt)], inc="VX")

    gVT.__exit__(None, None, None)
    gV.__exit__(None, None, None)
    gK.__exit__(None, None, None)

    gQ, psQ = psum_open("psQ", [128, TOK], F32)
    gS0, psS0 = psum_open("psS0", [128, TOK], F32)
    gS1, psS1 = psum_open("psS1", [128, TOK], F32)
    gT, psT = psum_open("psT", [128, TOK], BF16)
    gAV, psAV = psum_open("psAV", [128, 512], F32)
    psS = [psS0, psS1]

    mm_q = [0] * H
    vx_rope = [0] * H
    qr_last_scores = [0] * H
    pd_park = [0] * H
    psS_last_sx = [0, 0]
    af_last_vx = [0, 0]
    abf_last_mm = [0, 0]
    at_last_mm = [0, 0]
    psT_last_vx = 0
    psav_last_vx = 0
    rbuf_last_vx = [0, 0]
    c_of = {}

    cosq = cos_k[:, HALO:HALO + TOK]
    sinq = sinm_k[:, HALO:HALO + TOK]

    for h in range(H):
        if h < 2:
            w_st = [("MM", mm_v)]
        elif h == 2:
            w_st = []
        else:
            w_st = [("MM", mm_q[h - 3])]
        wd_q = gdma(stage_q[h % 3][:, :, :], wq_t[h], waits=w_st)

        def f_qproj(e, h=h, sl=stage_q[h % 3]):
            last = None
            for i in range(NI):
                for th in range(2):
                    last = e.matmul(psQ[:, 512 * th:512 * th + 512], sl[:, i, :],
                                    x_bf[:, i, HALO + 512 * th:HALO + 512 * th + 512],
                                    start=(i == 0), stop=(i == NI - 1))
            return last
        wq_waits = [("WD", wd_q)]
        if h > 0:
            wq_waits.append(("VX", vx_rope[h - 1]))
        mm_q[h] = op("tensor", f_qproj, waits=wq_waits, inc="MM")

        def f_rope(e, h=h):
            e.tensor_mul(qtmp1[:, :], psQ[:, :], cosq)
            e.tensor_mul(qtmp2[0:64, :], psQ[64:128, :], sinq[0:64, :])
            e.tensor_mul(qtmp2[64:128, :], psQ[0:64, :], sinq[64:128, :])
            return e.tensor_add(qr_ring[:, h % 3, :], qtmp1[:, :], qtmp2[:, :])
        rope_waits = [("MM", mm_q[h])]
        if h >= 3:
            rope_waits.append(("MM", qr_last_scores[h - 3]))
        vx_rope[h] = op("vector", f_rope, waits=rope_waits, inc="VX")

        for j in range(4):
            mm_sc = [0, 0]
            sx_sig = [0, 0]
            for w in range(2):
                koff = 256 * j + 256 * w

                def f_scores(e, h=h, j=j, w=w, koff=koff):
                    last = None
                    for qc in range(2):
                        qo = 256 * j + 128 * qc
                        last = e.matmul(psS[w][:, 512 * qc:512 * qc + 512],
                                        qr_ring[:, h % 3, qo:qo + 128],
                                        kr[:, koff:koff + 512],
                                        start=True, stop=True)
                    return last
                sc_waits = [("VX", vx_rope[h]), ("SX", psS_last_sx[w])]
                mm_sc[w] = op("tensor", f_scores, waits=sc_waits, inc="MM")

                def f_sig(e, h=h, j=j, w=w):
                    last = None
                    for qc in range(2):
                        last = e.activation(attn_f32[:, w, qc, :],
                                            psS[w][:, 512 * qc:512 * qc + 512],
                                            AF.Sigmoid, scale=SCALE,
                                            accum_out=rbuf[:, j % 2, 2 * w + qc:2 * w + qc + 1])
                    return last
                sig_waits = [("MM", mm_sc[w]), ("VX", af_last_vx[w]),
                             ("VX", rbuf_last_vx[j % 2])]
                sx_sig[w] = op("scalar", f_sig, waits=sig_waits, inc="SX")
                psS_last_sx[w] = sx_sig[w]

            vx_c1 = op("vector",
                       lambda e, j=j: e.tensor_sub(csub[:, j % 2, :],
                                                   rbuf[:, j % 2, :], corr[:, j, :]),
                       waits=[("SX", sx_sig[1])], inc="VX")
            vx_c2 = op("vector",
                       lambda e, j=j: e.reciprocal(crec[:, j % 2, :],
                                                   csub[:, j % 2, :]),
                       waits=[("VX", vx_c1)], inc="VX")
            vx_c = op("vector",
                      lambda e, j=j: e.tensor_mul(cbuf[:, j % 2, :],
                                                  crec[:, j % 2, :], alpha[:, j, :]),
                      waits=[("VX", vx_c2)], inc="VX")
            rbuf_last_vx[j % 2] = vx_c
            c_of[(h, j)] = vx_c

            mm_av_last = 0
            for w in range(2):
                def f_scale(e, h=h, j=j, w=w):
                    last = None
                    for qc in range(2):
                        last = e.tensor_scalar(attn_bfv[:, w, qc, :],
                                               attn_f32[:, w, qc, :],
                                               cbuf[:, j % 2, 2 * w + qc:2 * w + qc + 1],
                                               None, ALU.mult)
                    return last
                vx_scale = op("vector", f_scale,
                              waits=[("VX", vx_c), ("MM", abf_last_mm[w])], inc="VX")
                af_last_vx[w] = vx_scale

                def f_tp(e, w=w):
                    last = None
                    for ks in range(4):
                        for qc in range(2):
                            last = e.transpose(
                                psT[:, (2 * ks + qc) * 128:(2 * ks + qc + 1) * 128],
                                attn_bfv[:, w, qc, 128 * ks:128 * ks + 128],
                                ident[:, :])
                    return last
                mm_tp = op("tensor", f_tp,
                           waits=[("VX", vx_scale), ("VX", psT_last_vx)], inc="MM")
                abf_last_mm[w] = mm_tp

                vx_te = op("vector",
                           lambda e, w=w: e.tensor_copy(attnT[:, w, :], psT[:, 0:1024]),
                           waits=[("MM", mm_tp), ("MM", at_last_mm[w])], inc="VX")
                psT_last_vx = vx_te

                def f_av(e, j=j, w=w):
                    last = None
                    for ks in range(4):
                        last = e.matmul(psAV[:, 0:256], v_tm[:, 2 * j + 2 * w + ks, :],
                                        attnT[:, w, 256 * ks:256 * ks + 256],
                                        start=(w == 0 and ks == 0),
                                        stop=(w == 1 and ks == 3))
                    return last
                av_waits = [("VX", vx_te)]
                if w == 0:
                    av_waits.append(("VX", psav_last_vx))
                mm_av_last = op("tensor", f_av, waits=av_waits, inc="MM")

            at_last_mm[0] = mm_av_last
            at_last_mm[1] = mm_av_last
            qr_last_scores[h] = mm_sc[1]

            ave_waits = [("MM", mm_av_last)]
            if j == 0 and h > 0:
                ave_waits.append(("PD", pd_park[h - 1]))
            vx_ave = op("vector",
                        lambda e, j=j: e.tensor_copy(ao_chunk[:, 256 * j:256 * j + 256],
                                                     psAV[:, 0:256]),
                        waits=ave_waits, inc="VX")
            psav_last_vx = vx_ave

        pd_park[h] = sdma(attn_park[h], ao_chunk[:, :], waits=[("VX", psav_last_vx)])

    gAV.__exit__(None, None, None)
    gT.__exit__(None, None, None)
    gS1.__exit__(None, None, None)
    gS0.__exit__(None, None, None)
    gQ.__exit__(None, None, None)

    gWoA, psWoA = psum_open("psWoA", [128, TOK], F32)
    gWoB, psWoB = psum_open("psWoB", [128, TOK], F32)
    gSum, psSum = psum_open("psSum", [128, TOK], F32)
    gSq, psSq = psum_open("psSq", [128, TOK], F32)
    psWo_ = [psWoA, psWoB]

    pd_ard = 0
    for o in range(NI):
        w = [("MM", mm_q[H - 1])] if o == 0 else []
        pd_ard = sdma(attn_rd[:, o, :], attn_park[o], waits=w)

    mm_wo = [0] * NI
    vx_res1 = [0] * NI
    sx_sq = [0] * NI
    mm_stats = [0] * NI
    pd_r1park = [0] * NI
    wd_wo = [0] * NI
    pd_xres = [0] * NI

    for o in range(NI):
        w_st = [("MM", mm_q[H - 1])] if o < 3 else [("MM", mm_wo[o - 3])]
        wd_wo[o] = gdma(wo_stage[o % 3][:, :, :], wo_t[o], waits=w_st)

        xres_waits = []
        if o < 2:
            xres_waits += [("VX", vx_rope[H - 1]), ("MM", mm_q[H - 1])]
        else:
            xres_waits.append(("VX", vx_res1[o - 2]))
        pd_xres[o] = sdma(xres_ring[o % 2][:, :], xT_r[:, o, HALO:HALO + TOK],
                          waits=xres_waits)

        def f_wo(e, o=o, sl=wo_stage[o % 3]):
            last = None
            for i in range(NI):
                for th in range(2):
                    last = e.matmul(psWo_[o % 2][:, 512 * th:512 * th + 512], sl[:, i, :],
                                    attn_rd[:, i, 512 * th:512 * th + 512],
                                    start=(i == 0), stop=(i == NI - 1))
            return last
        wo_waits = [("WD", wd_wo[o])]
        if o == 0:
            wo_waits.append(("PD", pd_ard))
        if o > 1:
            wo_waits.append(("VX", vx_res1[o - 2]))
        mm_wo[o] = op("tensor", f_wo, waits=wo_waits, inc="MM")

        r1_waits = [("MM", mm_wo[o]), ("PD", pd_xres[o])]
        if o >= 2:
            r1_waits += [("PD", pd_r1park[o - 2]), ("MM", mm_stats[o - 2]),
                         ("SX", sx_sq[o - 2])]
        vx_res1[o] = op("vector",
                        lambda e, o=o: e.tensor_add(res1_ring[o % 2][:, :],
                                                    psWo_[o % 2][:, :], xres_ring[o % 2][:, :]),
                        waits=r1_waits, inc="VX")

        sq_waits = [("VX", vx_res1[o])]
        if o >= 2:
            sq_waits.append(("MM", mm_stats[o - 2]))
        sx_sq[o] = op("scalar",
                      lambda e, o=o: e.activation(sq1_ring[o % 2][:, :],
                                                  res1_ring[o % 2][:, :], AF.Square),
                      waits=sq_waits, inc="SX")

        def f_stats(e, o=o):
            for th in range(2):
                e.matmul(psSum[:, 512 * th:512 * th + 512], ones_f[:, :],
                         res1_ring[o % 2][:, 512 * th:512 * th + 512],
                         start=(o == 0), stop=(o == NI - 1))
            last = None
            for th in range(2):
                last = e.matmul(psSq[:, 512 * th:512 * th + 512], ones_b[:, :],
                                sq1_ring[o % 2][:, 512 * th:512 * th + 512],
                                start=(o == 0), stop=(o == NI - 1))
            return last
        stat_waits = [("VX", vx_res1[o]), ("SX", sx_sq[o])]
        if o == 0:
            stat_waits.append(("VX", vx_ones))
        mm_stats[o] = op("tensor", f_stats, waits=stat_waits, inc="MM")

        pd_r1park[o] = sdma(res1_park[o], res1_ring[o % 2][:, :],
                            waits=[("VX", vx_res1[o])])

    def f_coef1a(e):
        e.tensor_scalar(st_mean[:, :], psSum[:, :], INV_D, None, ALU.mult)
        e.tensor_scalar(st_msq[:, :], psSq[:, :], INV_D, None, ALU.mult)
        e.tensor_mul(st_var[:, :], st_mean[:, :], st_mean[:, :])
        e.tensor_sub(st_var[:, :], st_msq[:, :], st_var[:, :])
        return e.tensor_scalar(st_a[:, :], st_var[:, :], EPS, None, ALU.add)
    vx_c1a = op("vector", f_coef1a, waits=[("MM", mm_stats[NI - 1])], inc="VX")
    sx_c1 = op("scalar", lambda e: e.activation(st_var[:, :], st_a[:, :], AF.Sqrt),
               waits=[("VX", vx_c1a)], inc="SX")

    def f_coef1b(e):
        e.reciprocal(st_a[:, :], st_var[:, :])
        return e.tensor_mul(st_b[:, :], st_mean[:, :], st_a[:, :])
    vx_ab1 = op("vector", f_coef1b, waits=[("SX", sx_c1)], inc="VX")

    gSq.__exit__(None, None, None)
    gSum.__exit__(None, None, None)
    gWoB.__exit__(None, None, None)
    gWoA.__exit__(None, None, None)

    gU1p, psU1 = psum_open("psU1", [16, TOK], F32)
    gU3p, psU3 = psum_open("psU3", [16, TOK], F32)

    pd_r1r = [0] * NI
    vx_xln = [0] * NI
    sx_xbf = [0] * NI
    pd_xlnpark = [0] * NI
    mm_u13 = [0] * NI

    for o in range(NI):
        rr_waits = []
        if o < 2:
            rr_waits += [("MM", mm_stats[NI - 1]), ("SX", sx_sq[NI - 1])]
        else:
            rr_waits.append(("VX", vx_xln[o - 2]))
        pd_r1r[o] = sdma(res1_ring[o % 2][:, :], res1_park[o], waits=rr_waits)

        def f_apply(e, o=o):
            e.tensor_mul(qtmp1[:, :], res1_ring[o % 2][:, :], st_a[:, :])
            return e.tensor_sub(xlnf_ring[o % 2][:, :], qtmp1[:, :], st_b[:, :])
        ap_waits = [("PD", pd_r1r[o])]
        if o == 0:
            ap_waits.append(("VX", vx_ab1))
        if o >= 2:
            ap_waits.append(("PD", pd_xlnpark[o - 2]))
        vx_xln[o] = op("vector", f_apply, waits=ap_waits, inc="VX")

        xbf_waits = [("VX", vx_xln[o])]
        if o == 0:
            xbf_waits.append(("MM", mm_stats[NI - 1]))
        sx_xbf[o] = op("scalar",
                       lambda e, o=o: e.copy(xln_bf[:, o, :], xlnf_ring[o % 2][:, :]),
                       waits=xbf_waits, inc="SX")

        pd_xlnpark[o] = sdma(xln_park[o], xlnf_ring[o % 2][:, :],
                             waits=[("VX", vx_xln[o])])

        def f_u13(e, o=o):
            last = None
            for th in range(2):
                e.matmul(psU1[:, 512 * th:512 * th + 512], a13_bf[:, o, 0:16],
                         xln_bf[:, o, 512 * th:512 * th + 512],
                         start=(o == 0), stop=(o == NI - 1))
                last = e.matmul(psU3[:, 512 * th:512 * th + 512], a13_bf[:, o, 16:32],
                                xln_bf[:, o, 512 * th:512 * th + 512],
                                start=(o == 0), stop=(o == NI - 1))
            return last
        mm_u13[o] = op("tensor", f_u13, waits=[("SX", sx_xbf[o])], inc="MM")

    def f_u13e(e):
        e.tensor_scalar(u1_bf[:, :], psU1[:, :], 2.0, None, ALU.mult)
        return e.tensor_scalar(u3_bf[:, :], psU3[:, :], 2.0, None, ALU.mult)
    vx_u13 = op("vector", f_u13e, waits=[("MM", mm_u13[NI - 1])], inc="VX")

    gU3p.__exit__(None, None, None)
    gU1p.__exit__(None, None, None)

    gW2, psW2 = psum_open("psW2", [128, TOK], F32)
    gU2, psU2 = psum_open("psU2", [16, TOK], F32)
    gH1, psH1 = psum_open("psH1", [128, TOK], F32)
    gH3, psH3 = psum_open("psH3", [128, TOK], F32)

    mm_h = {}
    vx_g = {}
    sx_silu = {}
    slot_last_mm = {}

    def emit_h_phase(o_range, first_waits, prev_silu_sx, prev_g_vx, silu_pd_guard=0):
        for oi, o in enumerate(o_range):
            oo = o % NH1
            s1 = (2 * o) % 3
            s3 = (2 * o + 1) % 3
            wd1 = gdma(s13_ring[s1][:, :, :], w1_t[o],
                       waits=[("MM", slot_last_mm.get(("s13", s1), 0))])
            wd3 = gdma(s13_ring[s3][:, :, :], w3_t[o],
                       waits=[("MM", slot_last_mm.get(("s13", s3), 0))])

            def f_h1(e, o=o, s=s1):
                last = None
                for th in range(2):
                    for i in range(NI):
                        e.matmul(psH1[:, 512 * th:512 * th + 512],
                                 s13_ring[s][:, i, :],
                                 xln_bf[:, i, 512 * th:512 * th + 512],
                                 start=(i == 0), stop=False)
                    last = e.matmul(psH1[:, 512 * th:512 * th + 512],
                                    s13_ring[s][0:16, NI, :],
                                    u1_bf[:, 512 * th:512 * th + 512],
                                    start=False, stop=True)
                return last
            h1_waits = [("WD", wd1)]
            if oi > 0:
                h1_waits.append(("VX", vx_g[o_range[oi - 1]]))
            else:
                h1_waits += first_waits
                h1_waits += [("SX", prev_silu_sx), ("VX", prev_g_vx)]
            mm1 = op("tensor", f_h1, waits=h1_waits, inc="MM")
            slot_last_mm[("s13", s1)] = mm1

            def f_h3(e, o=o, s=s3):
                last = None
                for th in range(2):
                    for i in range(NI):
                        e.matmul(psH3[:, 512 * th:512 * th + 512],
                                 s13_ring[s][:, i, :],
                                 xln_bf[:, i, 512 * th:512 * th + 512],
                                 start=(i == 0), stop=False)
                    last = e.matmul(psH3[:, 512 * th:512 * th + 512],
                                    s13_ring[s][0:16, NI, :],
                                    u3_bf[:, 512 * th:512 * th + 512],
                                    start=False, stop=True)
                return last
            mm3 = op("tensor", f_h3, waits=[("WD", wd3)], inc="MM")
            slot_last_mm[("s13", s3)] = mm3
            mm_h[o] = mm3

            si_waits = [("MM", mm1)]
            if oi > 0:
                si_waits.append(("VX", vx_g[o_range[oi - 1]]))
            else:
                si_waits.append(("VX", prev_g_vx))
                if silu_pd_guard:
                    si_waits.append(("PD", silu_pd_guard))
            sx_silu[o] = op("scalar",
                            lambda e, o=o: e.activation(silu_sb[:, :], psH1[:, :], AF.Silu),
                            waits=si_waits, inc="SX")

            g_waits = [("SX", sx_silu[o]), ("MM", mm3)]
            if oi == 0:
                g_waits += first_waits
            vx_g[o] = op("vector",
                         lambda e, oo=oo: e.tensor_mul(g_sb[:, oo, :], silu_sb[:, :],
                                                       psH3[:, :]),
                         waits=g_waits, inc="VX")

    for k in range(3):
        slot_last_mm[("s13", k)] = 0
    p5_first = [("PD", pd_xlnpark[NI - 1]), ("MM", mm_u13[NI - 1]), ("VX", vx_u13)]
    slot_last_mm[("s13", 0)] = mm_wo[NI - 1]
    slot_last_mm[("s13", 1)] = mm_wo[NI - 1]
    slot_last_mm[("s13", 2)] = mm_wo[NI - 1]
    emit_h_phase(list(range(NH1)), p5_first, prev_silu_sx=0, prev_g_vx=0)

    def f_u2a(e):
        last = None
        for th in range(2):
            for i in range(NH1):
                last = e.matmul(psU2[:, 512 * th:512 * th + 512], a2_bf[:, i, :],
                                g_sb[:, i, 512 * th:512 * th + 512],
                                start=(i == 0), stop=(i == NH1 - 1))
        return last
    mm_u2a = op("tensor", f_u2a, waits=[("VX", vx_g[NH1 - 1])], inc="MM")
    vx_u2a = op("vector",
                lambda e: e.tensor_scalar(u2a[:, :], psU2[:, :], 2.0, None, ALU.mult),
                waits=[("MM", mm_u2a)], inc="VX")

    vx_part = [0] * NI
    mm_w2a = [0] * NI
    pd_pp = [0] * NI
    s2a_last = [mm_h[NH1 - 1], mm_h[NH1 - 1]]

    for o in range(NI):
        wd2 = gdma(s2a_ring[o % 2][:, :, :], w2a_t[o], waits=[("MM", s2a_last[o % 2])])

        xr_waits = [("VX", vx_g[NH1 - 1])] if o == 0 else [("VX", vx_part[o - 1])]
        pd_xr = sdma(p6_xstage[:, :], xln_park[o], waits=xr_waits)

        def f_w2a(e, o=o, sl=s2a_ring[o % 2]):
            last = None
            for th in range(2):
                for i in range(NH1):
                    last = e.matmul(psW2[:, 512 * th:512 * th + 512], sl[:, i, :],
                                    g_sb[:, i, 512 * th:512 * th + 512],
                                    start=(i == 0), stop=(i == NH1 - 1))
            return last
        w2_waits = [("WD", wd2)]
        if o > 0:
            w2_waits.append(("VX", vx_part[o - 1]))
        mm_w2a[o] = op("tensor", f_w2a, waits=w2_waits, inc="MM")
        s2a_last[o % 2] = mm_w2a[o]

        vx_part[o] = op("vector",
                        lambda e: e.tensor_add(p6_xstage[:, :], psW2[:, :],
                                               p6_xstage[:, :]),
                        waits=[("MM", mm_w2a[o]), ("PD", pd_xr)], inc="VX")
        pd_pp[o] = sdma(part_park[o], p6_xstage[:, :], waits=[("VX", vx_part[o])])

    p7_first = [("MM", mm_w2a[NI - 1])]
    slot_last_mm[("s13", 0)] = mm_w2a[NI - 1]
    slot_last_mm[("s13", 1)] = mm_w2a[NI - 1]
    slot_last_mm[("s13", 2)] = mm_w2a[NI - 1]
    emit_h_phase(list(range(NH1, NHID)), p7_first,
                 prev_silu_sx=sx_silu[NH1 - 1], prev_g_vx=vx_g[NH1 - 1],
                 silu_pd_guard=pd_pp[NI - 1])

    def f_u2d(e):
        last = None
        for th in range(2):
            for i in range(NH1):
                last = e.matmul(psU2[:, 512 * th:512 * th + 512], a2_bf[:, NH1 + i, :],
                                g_sb[:, i, 512 * th:512 * th + 512],
                                start=(i == 0), stop=(i == NH1 - 1))
        return last
    mm_u2d = op("tensor", f_u2d, waits=[("VX", vx_g[NHID - 1])], inc="MM")
    vx_u2 = op("vector",
               lambda e: e.scalar_tensor_tensor(u2_bf[:, :], psU2[:, :], 2.0,
                                                u2a[:, :], ALU.mult, ALU.add),
               waits=[("MM", mm_u2d), ("VX", vx_u2a)], inc="VX")

    gH3.__exit__(None, None, None)
    gH1.__exit__(None, None, None)
    gSum2, psSum2 = psum_open("psSum2", [128, TOK], F32)
    gSq2, psSq2 = psum_open("psSq2", [128, TOK], F32)

    vx_res2 = [0] * NI
    mm_w2b = [0] * NI
    sx_sq2 = [0] * NI
    mm_stats2 = [0] * NI
    pd_r2park = [0] * NI
    s2b_last = [mm_h[NHID - 1], mm_h[NHID - 1]]

    for o in range(NI):
        wd2 = gdma(s2b_ring[o % 2][:, :, :], w2b_t[o], waits=[("MM", s2b_last[o % 2])])

        pr_waits = [("MM", mm_h[NHID - 1])] if o < 2 else [("VX", vx_res2[o - 2])]
        pd_pr = sdma(p8_xstage[o % 2][:, :], part_park[o], waits=pr_waits)

        def f_w2b(e, o=o, sl=s2b_ring[o % 2]):
            last = None
            for th in range(2):
                for i in range(NH1):
                    e.matmul(psW2[:, 512 * th:512 * th + 512], sl[:, i, :],
                             g_sb[:, i, 512 * th:512 * th + 512],
                             start=(i == 0), stop=False)
                last = e.matmul(psW2[:, 512 * th:512 * th + 512],
                                sl[0:16, NH1, :],
                                u2_bf[:, 512 * th:512 * th + 512],
                                start=False, stop=True)
            return last
        w2b_waits = [("WD", wd2), ("VX", vx_u2)]
        if o > 0:
            w2b_waits.append(("VX", vx_res2[o - 1]))
        mm_w2b[o] = op("tensor", f_w2b, waits=w2b_waits, inc="MM")
        s2b_last[o % 2] = mm_w2b[o]

        r2_waits = [("MM", mm_w2b[o]), ("PD", pd_pr)]
        if o >= 2:
            r2_waits += [("PD", pd_r2park[o - 2]), ("MM", mm_stats2[o - 2]),
                         ("SX", sx_sq2[o - 2])]
        vx_res2[o] = op("vector",
                        lambda e, o=o: e.tensor_add(p8_res2[o % 2][:, :], psW2[:, :],
                                                    p8_xstage[o % 2][:, :]),
                        waits=r2_waits, inc="VX")

        sq2_waits = [("VX", vx_res2[o])]
        if o >= 2:
            sq2_waits.append(("MM", mm_stats2[o - 2]))
        sx_sq2[o] = op("scalar",
                       lambda e, o=o: e.activation(p8_sq[o % 2][:, :],
                                                   p8_res2[o % 2][:, :], AF.Square),
                       waits=sq2_waits, inc="SX")

        def f_stats2(e, o=o):
            for th in range(2):
                e.matmul(psSum2[:, 512 * th:512 * th + 512], ones_f[:, :],
                         p8_res2[o % 2][:, 512 * th:512 * th + 512],
                         start=(o == 0), stop=(o == NI - 1))
            last = None
            for th in range(2):
                last = e.matmul(psSq2[:, 512 * th:512 * th + 512], ones_b[:, :],
                                p8_sq[o % 2][:, 512 * th:512 * th + 512],
                                start=(o == 0), stop=(o == NI - 1))
            return last
        mm_stats2[o] = op("tensor", f_stats2,
                          waits=[("VX", vx_res2[o]), ("SX", sx_sq2[o])], inc="MM")

        pd_r2park[o] = sdma(res2_park[o], p8_res2[o % 2][:, :],
                            waits=[("VX", vx_res2[o])])

    def f_coef2a(e):
        e.tensor_scalar(st2_mean[:, :], psSum2[:, :], INV_D, None, ALU.mult)
        e.tensor_scalar(st2_msq[:, :], psSq2[:, :], INV_D, None, ALU.mult)
        e.tensor_mul(st2_var[:, :], st2_mean[:, :], st2_mean[:, :])
        e.tensor_sub(st2_var[:, :], st2_msq[:, :], st2_var[:, :])
        return e.tensor_scalar(st2_a[:, :], st2_var[:, :], EPS, None, ALU.add)
    vx_c2a = op("vector", f_coef2a, waits=[("MM", mm_stats2[NI - 1])], inc="VX")
    sx_c2 = op("scalar", lambda e: e.activation(st2_var[:, :], st2_a[:, :], AF.Sqrt),
               waits=[("VX", vx_c2a)], inc="SX")

    def f_coef2b(e):
        e.reciprocal(st2_a[:, :], st2_var[:, :])
        return e.tensor_mul(st2_b[:, :], st2_mean[:, :], st2_a[:, :])
    vx_ab2 = op("vector", f_coef2b, waits=[("SX", sx_c2)], inc="VX")

    vx_yout = [0] * NI
    pd_out = [0] * NI
    for o in range(NI):
        rr_waits = []
        if o < 2:
            rr_waits.append(("MM", mm_w2b[NI - 1]))
        else:
            rr_waits.append(("VX", vx_yout[o - 2]))
        pd_rr = sdma(p9_rring[o % 2][:, :], res2_park[o], waits=rr_waits)

        def f_y(e, o=o):
            e.tensor_mul(p9_yout[o % 2][:, :], p9_rring[o % 2][:, :], st2_a[:, :])
            return e.tensor_sub(p9_yout[o % 2][:, :], p9_yout[o % 2][:, :], st2_b[:, :])
        y_waits = [("PD", pd_rr)]
        if o == 0:
            y_waits.append(("VX", vx_ab2))
        if o >= 2:
            y_waits.append(("PD", pd_out[o - 2]))
        vx_yout[o] = op("vector", f_y, waits=y_waits, inc="VX")

        pd_out[o] = sdma(outT[o], p9_yout[o % 2][:, :], waits=[("VX", vx_yout[o])])

    final_pd = pd_out[NI - 1]
    op("sync", lambda e: e.nop(), waits=[("PD", final_pd)])

    gSq2.__exit__(None, None, None)
    gSum2.__exit__(None, None, None)
    gU2.__exit__(None, None, None)
    gW2.__exit__(None, None, None)

    with nc.Block() as block:
        for eng_name, starter in (("tensor", block.tensor), ("vector", block.vector),
                                  ("scalar", block.scalar), ("sync", block.sync),
                                  ("gpsimd", block.gpsimd)):
            ops_e = [o for o in OPS if o[0] == eng_name]
            if not ops_e:
                continue

            def body(e, ops_e=ops_e):
                for (_, waits, fn, inc) in ops_e:
                    for (sk, val) in waits:
                        e.wait_ge(SEMS[sk], val)
                    inst = fn(e)
                    if inc is not None:
                        inst.then_inc(SEMS[inc], 16 if inc in ("WD", "PD") else 1)
            starter(body)

    return nc



def _tile_w(W, B=None):
    out_dim, in_dim = W.shape
    O, I = out_dim // 128, in_dim // 128
    t = np.ascontiguousarray(
        W.reshape(O, 128, I, 128).transpose(0, 3, 2, 1).astype(np.float32))
    if B is None:
        return t
    bt = B.reshape(O, 128, R).transpose(0, 2, 1).astype(np.float32)
    pad = np.zeros((O, 128, 1, 128), np.float32)
    pad[:, :R, 0, :] = bt
    return np.ascontiguousarray(np.concatenate([t, pad], axis=2))


def _rope_tables():
    base = 10000.0
    inv_freq = 1.0 / base ** (np.arange(0, DH, 2, dtype=np.float32) / DH)
    t = np.arange(S, dtype=np.float32)
    f = np.outer(t, inv_freq)
    cos = np.zeros((S, DH), dtype=np.float32)
    sin = np.zeros((S, DH), dtype=np.float32)
    cos[:, 0::2] = np.cos(f)
    cos[:, 1::2] = np.cos(f)
    sin[:, 0::2] = np.sin(f)
    sin[:, 1::2] = np.sin(f)
    return cos, sin


def kernel(L0_attn_x, L0_attn_Wq, L0_attn_Wk, L0_attn_Wv, L0_attn_Wo,
           L0_ffn_w1, L0_ffn_w3, L0_ffn_w2,
           L0_ffn_lora_A1, L0_ffn_lora_B1, L0_ffn_lora_A3, L0_ffn_lora_B3,
           L0_ffn_lora_A2, L0_ffn_lora_B2):
    x = np.asarray(L0_attn_x, dtype=np.float32).reshape(S, D)
    Wq = np.asarray(L0_attn_Wq, np.float32)
    Wk = np.asarray(L0_attn_Wk, np.float32)
    Wv = np.asarray(L0_attn_Wv, np.float32)
    Wo = np.asarray(L0_attn_Wo, np.float32)
    w1 = np.asarray(L0_ffn_w1, np.float32)
    w3 = np.asarray(L0_ffn_w3, np.float32)
    w2 = np.asarray(L0_ffn_w2, np.float32)
    A1 = np.asarray(L0_ffn_lora_A1, np.float32)
    B1 = np.asarray(L0_ffn_lora_B1, np.float32)
    A3 = np.asarray(L0_ffn_lora_A3, np.float32)
    B3 = np.asarray(L0_ffn_lora_B3, np.float32)
    A2 = np.asarray(L0_ffn_lora_A2, np.float32)
    B2 = np.asarray(L0_ffn_lora_B2, np.float32)

    wq_t = _tile_w(Wq)
    wk_t = _tile_w(Wk)
    wv_t = _tile_w(Wv)
    wo_t = _tile_w(Wo)
    w1_t = _tile_w(w1, B1)
    w3_t = _tile_w(w3, B3)
    w2a_t = _tile_w(w2[:, :NH1 * 128])
    w2b_t = _tile_w(w2[:, NH1 * 128:], B2)
    A13 = np.concatenate([A1, A3], axis=0)
    a13_t = np.ascontiguousarray(A13.reshape(2 * R, NI, 128).transpose(2, 1, 0))
    a2_t = np.ascontiguousarray(A2.reshape(R, NHID, 128).transpose(2, 1, 0))
    ident = np.eye(128, dtype=np.float32)

    cos_g, sin_g = _rope_tables()
    lin = np.linspace(0.0, 1.0, 256, dtype=np.float32)

    nc = build_nc()
    in_maps = []
    for c in range(NCORES):
        g0 = 1024 * c - HALO
        xh = np.zeros((TOKH, D), np.float32)
        lo, hi = max(0, g0), min(S, g0 + TOKH)
        xh[lo - g0:hi - g0] = x[lo:hi]
        xTc = np.ascontiguousarray(xh.T)

        pos = g0 + np.arange(TOKH)
        valid = (pos >= 0) & (pos < S)
        pc = np.clip(pos, 0, S - 1)
        cos_c = np.where(valid[None, :], cos_g[pc].T, 0.0).astype(np.float32)
        sin_c = np.where(valid[None, :], sin_g[pc].T, 0.0).astype(np.float32)
        sinm_c = sin_c.copy()
        sinm_c[:64] = -sinm_c[:64]

        al = np.zeros((128, 4, 2, 2), np.float32)
        co = np.zeros((128, 4, 2, 2), np.float32)
        p = np.arange(128)
        for j in range(4):
            for qc in range(2):
                u = lin[128 * qc + p]
                al[:, j, 0, qc] = 1.0 - u
                al[:, j, 1, qc] = u
        if c == 0:
            al[:, 0, 0, :] = 0.0
            al[:, 0, 1, :] = 1.0
            co[:, 0, 0, :] = 128.0
        if c == NCORES - 1:
            co[:, 3, 1, :] = 128.0

        in_maps.append({
            "xT": xTc,
            "wq_t": wq_t, "wk_t": wk_t, "wv_t": wv_t, "wo_t": wo_t,
            "w1_t": w1_t, "w3_t": w3_t, "w2a_t": w2a_t, "w2b_t": w2b_t,
            "a13_t": a13_t, "a2_t": a2_t,
            "cos_t": cos_c, "sinm_t": sinm_c,
            "alpha_t": np.ascontiguousarray(al.reshape(128, 16)),
            "corr_t": np.ascontiguousarray(co.reshape(128, 16)),
            "ident_t": ident,
        })

    res = run_bass_kernel_spmd(nc, in_maps, core_ids=list(range(NCORES)))
    parts = []
    for c in range(NCORES):
        oc = np.asarray(res.results[c]["outT"])
        parts.append(oc.reshape(D, TOK).T)
    out = np.concatenate(parts, axis=0).reshape(1, S, D)
    return np.ascontiguousarray(out.astype(np.float32))
